# version 08 (final)
# baseline (speedup 1.0000x reference)
import numpy as np
import ml_dtypes
import concourse.bass as bass
import concourse.mybir as mybir
from concourse.bass_utils import run_bass_kernel_spmd

F32 = mybir.dt.float32
BF16 = mybir.dt.bfloat16
AF = mybir.ActivationFunctionType
ALU = mybir.AluOpType
AX = mybir.AxisListType

S = 2048
E = 2048
NT = S // 128
NE = E // 128
NH = 16
NKV = 4
HD = 128
NEG = -1e30


class Eng:
    def __init__(self, nc, name):
        self.sem = nc.alloc_semaphore(name + "_sem")
        self.n = 0
        self.ops = []

    def emit(self, fn, inc=0):
        self.ops.append((fn, inc, self.sem))
        self.n += inc
        return self.n

    def wait(self, other, val):
        assert val <= other.n, f"wait on future count {val} > {other.n}"
        sem = other.sem
        self.ops.append((lambda e, s=sem, v=val: e.wait_ge(s, v), 0, None))

    def replay(self, e):
        for fn, inc, sem in self.ops:
            inst = fn(e)
            if inc:
                inst.then_inc(sem, inc)


def build_program(stages=6, dump=None):
    nc = bass.Bass("TRN2", target_bir_lowering=False,
                   dynamic_dma_scratch_size=2048)

    x_d = nc.dram_tensor("x", [S, E], F32, kind="ExternalInput").ap()
    wffnT_d = nc.dram_tensor("wffnT", [E, E], F32, kind="ExternalInput").ap()
    wq_d = nc.dram_tensor("wq", [E, E], F32, kind="ExternalInput").ap()
    wkv_d = nc.dram_tensor("wkv", [E, 1024], F32, kind="ExternalInput").ap()
    wo_d = nc.dram_tensor("wo", [E, E], F32, kind="ExternalInput").ap()
    scale_d = nc.dram_tensor("fscale", [1, E], F32, kind="ExternalInput").ap()
    fbias_d = nc.dram_tensor("fbias", [1, E], F32, kind="ExternalInput").ap()
    bq_d = nc.dram_tensor("bq", [1, E], F32, kind="ExternalInput").ap()
    bkv_d = nc.dram_tensor("bkv", [1, 1024], F32, kind="ExternalInput").ap()
    bo_d = nc.dram_tensor("bo", [1, E], F32, kind="ExternalInput").ap()
    cosin_d = nc.dram_tensor("cosin", [128, S], F32, kind="ExternalInput").ap()
    masks_d = nc.dram_tensor("masks", [128, 384], BF16, kind="ExternalInput").ap()
    idn_d = nc.dram_tensor("idn", [128, 128], BF16, kind="ExternalInput").ap()
    out_d = nc.dram_tensor("out", [S, E], F32, kind="ExternalOutput").ap()

    xnT = nc.alloc_sbuf_tensor("xnT", [128, NE * S], BF16)
    QY = nc.alloc_sbuf_tensor("QY", [128, NH * S], BF16)
    KT = nc.alloc_sbuf_tensor("KT", [128, NKV * S], BF16)
    VT = nc.alloc_sbuf_tensor("VT", [128, NKV * S], BF16)
    AR = nc.alloc_sbuf_tensor("AR", [128, 7168], F32)
    COSIN = nc.alloc_sbuf_tensor("COSIN", [128, S], F32)
    ROWS = nc.alloc_sbuf_tensor("ROWS", [128, 6144], BF16)
    MST = nc.alloc_sbuf_tensor("MST", [32, 128], BF16)
    SLAB = nc.alloc_sbuf_tensor("SLAB", [128, 4096], BF16)
    XNTB = nc.alloc_sbuf_tensor("XNTB", [128, 2048], BF16)
    MASKS = nc.alloc_sbuf_tensor("MASKS", [128, 384], BF16)
    IDN = nc.alloc_sbuf_tensor("IDN", [128, 128], BF16)
    MS = nc.alloc_sbuf_tensor("MS", [128, 32], F32)
    MSB = nc.alloc_sbuf_tensor("MSB", [128, 32], BF16)
    SM = nc.alloc_sbuf_tensor("SM", [128, 16], F32)
    CONST = nc.alloc_sbuf_tensor("CONST", [128, 4], F32)

    ps = [nc.alloc_psum_tensor(f"ps{i}", [128, 512], F32) for i in range(8)]
    psb = [p[:, :].bitcast(BF16) for p in ps]

    AXS = [AR[:, 0:2048], AR[:, 2048:4096]]
    ASTG = [AR[:, 4096:5120], AR[:, 5120:6144]]
    ASREP = AR[:, 6144:7168].bitcast(BF16)
    ASDR = AR[:, 6144:7168].bitcast(BF16)
    AT1 = AR[:, 0:512]
    AT2 = AR[:, 512:1024]
    AWQ = [AR[:, 1024:2048].bitcast(BF16), AR[:, 2048:3072].bitcast(BF16)]
    AWO = [AR[:, 4096:5120].bitcast(BF16), AR[:, 5120:6144].bitcast(BF16)]
    ASC = AR[:, 0:672]
    AEX = AR[:, 1024:1696]
    PBUF = SLAB[:, 0:768]
    PTB = SLAB[:, 768:1536]
    R_bq = ROWS[0:1, 0:2048]
    R_bo = ROWS[0:1, 2048:4096]
    R_ones0 = ROWS[0:1, 4096:4608]
    R_m = KT[0:1, 0:2048]
    R_sd = KT[0:1, 2048:4096]
    R_bkv = ROWS[64:65, 0:1024]
    R_scale = ROWS[64:65, 1024:3072]
    R_bias = ROWS[64:65, 3072:5120]
    R_ones64 = ROWS[64:65, 5120:5632]

    sy = Eng(nc, "sy")
    gp = Eng(nc, "gp")
    te = Eng(nc, "te")
    ve = Eng(nc, "ve")
    se = Eng(nc, "se")

    ps_free = [[] for _ in range(8)]

    def te_wait_free(bank):
        for eng, cnt in ps_free[bank]:
            if cnt > 0:
                te.wait(eng, cnt)

    sy.emit(lambda e: e.dma_start(COSIN[:, :], cosin_d[:, :]), 16)
    sy.emit(lambda e: e.dma_start(MASKS[:, :], masks_d[:, :]), 16)
    t_pro = sy.emit(lambda e: e.dma_start(IDN[:, :], idn_d[:, :]), 16)
    sy.emit(lambda e: e.dma_start(AR[0:1, 0:2048], scale_d[0:1, :]), 16)
    sy.emit(lambda e: e.dma_start(AR[0:1, 2048:4096], fbias_d[0:1, :]), 16)
    t_w1 = sy.emit(lambda e: e.dma_start(AR[0:1, 4096:6144], bq_d[0:1, :]), 16)
    ve.emit(lambda e: e.memset(R_ones0, 1.0))
    ve.emit(lambda e: e.memset(R_ones64, 1.0))
    ve.emit(lambda e: e.memset(CONST[:, 0:1], 1e-5))
    t_const = ve.emit(lambda e: e.memset(CONST[:, 1:2], 1.0 / E), 1)
    se.wait(sy, t_w1)
    se.emit(lambda e: e.activation(R_scale, AR[0:1, 0:2048], AF.Copy, scale=0.5))
    se.emit(lambda e: e.activation(R_bias, AR[0:1, 2048:4096], AF.Copy, scale=0.5))
    t_w1c = se.emit(lambda e: e.activation(R_bq, AR[0:1, 4096:6144], AF.Copy, scale=0.08838834764831845), 1)
    sy.wait(se, t_w1c)
    sy.emit(lambda e: e.dma_start(AR[0:1, 0:1024], bkv_d[0:1, :]), 16)
    t_w2 = sy.emit(lambda e: e.dma_start(AR[0:1, 1024:3072], bo_d[0:1, :]), 16)
    se.wait(sy, t_w2)
    se.emit(lambda e: e.copy(R_bkv, AR[0:1, 0:1024]))
    t_rows = se.emit(lambda e: e.copy(R_bo, AR[0:1, 1024:3072]), 1)

    def build_srep(dep_eng, dep_tok):
        te.wait(se, t_rows)
        if dep_eng is not None:
            te.wait(dep_eng, dep_tok)
        toks = []
        for i in range(4):
            b = 4 + i
            te_wait_free(b)
            toks.append(te.emit(lambda e, b=b, i=i: e.matmul(
                ps[b][:, 0:512], R_ones64[0:1, 0:128],
                R_scale[0:1, i * 512:(i + 1) * 512], start=True, stop=True), 1))
        last = 0
        for i in range(4):
            b = 4 + i
            ve.wait(te, toks[i])
            last = ve.emit(lambda e, b=b, i=i: e.tensor_copy(
                ASREP[:, i * 512:(i + 1) * 512], ps[b][:, 0:512]), 1)
            ps_free[b] = [(ve, last)]
        return last

    ln_war = {"ve": 0}

    def ln_stats(src_ap, sq_bank, mcol, rcol, deps):
        for eng, tok in deps:
            if tok:
                se.wait(eng, tok)
        if ln_war["ve"]:
            se.wait(ve, ln_war["ve"])
        for eng_f, cnt_f in ps_free[sq_bank]:
            if cnt_f:
                se.wait(eng_f, cnt_f)
        t_sq = 0
        for c in range(4):
            t_sq = se.emit(lambda e, c=c: e.activation(
                ps[sq_bank][:, 0:512], src_ap[:, c * 512:(c + 1) * 512],
                AF.Square, accum_out=SM[:, 8 + c:9 + c]), 1 if c == 3 else 0)
        ps_free[sq_bank] = [(se, t_sq)]
        for eng, tok in deps:
            if tok:
                ve.wait(eng, tok)
        t1 = ve.emit(lambda e: e.reduce_sum(SM[:, 12:13], src_ap[:, :], axis=AX.X), 1)
        ve.wait(ve, t1)
        t2 = ve.emit(lambda e: e.tensor_scalar(SM[:, mcol:mcol + 1], SM[:, 12:13],
                                               CONST[:, 1:2], None, ALU.mult), 1)
        ve.wait(se, t_sq)
        t3 = ve.emit(lambda e: e.reduce_sum(SM[:, 13:14], SM[:, 8:12], axis=AX.X), 1)
        ve.wait(ve, t3)
        t4 = ve.emit(lambda e: e.tensor_scalar(SM[:, 13:14], SM[:, 13:14],
                                               CONST[:, 1:2], None, ALU.mult), 1)
        ve.wait(ve, t2)
        ve.wait(ve, t4)
        t5 = ve.emit(lambda e: e.tensor_tensor(SM[:, 14:15], SM[:, mcol:mcol + 1],
                                               SM[:, mcol:mcol + 1], ALU.mult), 1)
        ve.wait(ve, t5)
        t_var = ve.emit(lambda e: e.tensor_sub(SM[:, 13:14], SM[:, 13:14],
                                               SM[:, 14:15]), 1)
        ln_war["ve"] = t_var
        se.wait(ve, t_var)
        t_sd = se.emit(lambda e: e.activation(SM[:, 15:16], SM[:, 13:14],
                                              AF.Sqrt, bias=CONST[:, 0:1]), 1)
        ve.wait(se, t_sd)
        t_r = ve.emit(lambda e: e.reciprocal(SM[:, rcol:rcol + 1], SM[:, 15:16]), 1)
        ve.wait(ve, t_r)
        return t_r, t_sq, t_sd

    def transpose_tile(src_ap, dst, dst_col, dep_eng, dep_tok, banks=(4, 5)):
        te.wait(dep_eng, dep_tok)
        tok_half = {}
        for half in range(2):
            b = banks[half]
            te_wait_free(b)
            t = 0
            for i in range(8):
                blk = half * 8 + i
                t = te.emit(lambda e, b=b, i=i, blk=blk: e.transpose(
                    psb[b][:, i * 128:(i + 1) * 128],
                    src_ap[:, blk * 128:(blk + 1) * 128], IDN[:, :]),
                    1 if i == 7 else 0)
            tok_half[half] = t
        last = 0
        for half in range(2):
            b = banks[half]
            ve.wait(te, tok_half[half])
            for i in range(8):
                blk = half * 8 + i
                last = ve.emit(lambda e, b=b, i=i, blk=blk: e.tensor_copy(
                    dst[:, dst_col(blk):dst_col(blk) + 128],
                    psb[b][:, i * 128:(i + 1) * 128]), 1 if i == 7 else 0)
            ps_free[b] = [(ve, last)]
        return last

    def load_wffn(extra_dep):
        stg_free = [extra_dep, extra_dep]
        last = 0
        for k in range(NE):
            for h2 in range(2):
                sl = (k * 2 + h2) % 2
                eng, cnt = stg_free[sl]
                if eng is not None and cnt:
                    sy.wait(eng, cnt)
                t_ld = sy.emit(lambda e, k=k, h2=h2, sl=sl: e.dma_start(
                    ASTG[sl][:, :], wffnT_d[k * 128:(k + 1) * 128,
                                            h2 * 1024:(h2 + 1) * 1024]), 16)
                ve.wait(sy, t_ld)
                last = ve.emit(lambda e, k=k, h2=h2, sl=sl: e.tensor_tensor(
                    QY[:, k * 2048 + h2 * 1024:k * 2048 + (h2 + 1) * 1024],
                    ASTG[sl][:, :], ASREP[:, h2 * 1024:(h2 + 1) * 1024],
                    ALU.mult), 1)
                stg_free[sl] = (ve, last)
        return last

    def ffn_tile(t, x_ap, out_ap, out_in1_ap, w_tok, xnt_tok):
        pair = (0, 1) if t % 2 == 0 else (2, 3)
        te.wait(ve, w_tok)
        te.wait(ve, xnt_tok)
        te.wait(se, t_rows)
        banks = [pair[0], pair[0], pair[1], pair[1]]
        offs = [0, 0, 0, 0]
        quad = (0, 1, 2, 3)
        for b in quad:
            te_wait_free(b)
        for nn in range(4):
            te.emit(lambda e, nn=nn: e.matmul(
                ps[quad[nn]][:, 0:512], R_ones64[0:1, 0:128],
                R_bias[0:1, nn * 512:(nn + 1) * 512], start=True, stop=False))
        t_mm = 0
        for k in range(NE):
            for nn in range(4):
                t_mm = te.emit(lambda e, nn=nn, k=k: e.matmul(
                    ps[quad[nn]][:, 0:512], XNTB[:, k * 128:(k + 1) * 128],
                    QY[:, k * 2048 + nn * 512:k * 2048 + nn * 512 + 512],
                    start=False, stop=(k == NE - 1)),
                    1 if (k == NE - 1 and nn == 3) else 0)
        last = 0
        for nn in range(4):
            ve.wait(te, t_mm)
            last = ve.emit(lambda e, nn=nn: e.tensor_add(
                out_ap[:, nn * 512:(nn + 1) * 512], ps[quad[nn]][:, 0:512],
                out_in1_ap[:, nn * 512:(nn + 1) * 512]), 1)
            ps_free[quad[nn]] = [(ve, last)]
        return last

    def _finish_debug():
        if dump is not None:
            se.wait(ve, ve.n)
            se.wait(te, te.n)
            se.wait(sy, sy.n)
            src, ncols = dump_map[dump]
            ntiles = ncols // 2048
            tok = se.n
            dtok = 0
            for i in range(ntiles):
                se.wait(gp, dtok)
                tok = se.emit(lambda e, i=i, src=src: e.copy(
                    AXS[0][:, :], src[:, i * 2048:(i + 1) * 2048]), 1)
                gp.wait(se, tok)
                dtok = gp.emit(lambda e, i=i: e.dma_start(
                    out_d[i * 128:(i + 1) * 128, :], AXS[0][:, :]), 16)
            gp.wait(gp, dtok)
            return
        t_end = gp.emit(lambda e: e.dma_start(out_d[0:128, :], AR[:, 0:2048]), 16)
        gp.wait(gp, t_end)

    def _assemble():
        with nc.Block() as block:
            @block.sync
            def _(e):
                sy.replay(e)

            @block.tensor
            def _(e):
                te.replay(e)

            @block.vector
            def _(e):
                ve.replay(e)

            @block.scalar
            def _(e):
                se.replay(e)

            @block.gpsimd
            def _(e):
                gp.replay(e)
        return nc

    dump_map = {"xnT": (xnT, NE * S), "KT": (KT, NKV * S), "VT": (VT, NKV * S),
                "QY": (QY, NH * S)}
    if stages <= 0:
        gp.wait(se, t_rows)
        _finish_debug()
        return _assemble()

    te.wait(sy, t_pro)
    ve.wait(sy, t_pro)
    se.wait(ve, t_const)
    gp.wait(sy, t_pro)
    t_srep = build_srep(None, 0)
    w1_tok = load_wffn((se, t_rows))

    x_free = [[(se, t_rows)], [(se, t_rows)]]
    u1 = [dict() for _ in range(NT)]
    for eng, tok in x_free[0]:
        sy.wait(eng, tok)
    u1[0]["x"] = sy.emit(lambda e: e.dma_start(AXS[0][:, :], x_d[0:128, :]), 16)
    t_xnT_done = 0
    for r in range(NT + 1):
        if r >= 1:
            t = r - 1
            sl = t % 2
            t_tr1 = transpose_tile(SLAB[:, 0:2048], XNTB, lambda b: b * 128,
                                   ve, u1[t]["xn1"])
            u1[t]["tr1te"] = te.n
            t_x1 = ffn_tile(t, AXS[sl], AXS[sl], AXS[sl], w1_tok, t_tr1)
            sq_bank = 6 if t % 2 == 0 else 7
            t_r2, t_sq2, t_sd2 = ln_stats(AXS[sl], sq_bank, 2, 3, [(ve, t_x1)])
            ve.emit(lambda e, t=t: e.tensor_copy(MS[:, t:t + 1], SM[:, 2:3]))
            ve.emit(lambda e, t=t: e.tensor_copy(MS[:, 16 + t:17 + t],
                                                 SM[:, 15:16]), 1)
            t_xn2 = ve.emit(lambda e, sl=sl: e.tensor_scalar(
                SLAB[:, 2048:4096], AXS[sl][:, :], SM[:, 2:3], SM[:, 3:4],
                ALU.subtract, ALU.mult), 1)
            x_free[sl] = [(ve, t_xn2), (se, t_sq2)]
            t_xnT_done = transpose_tile(SLAB[:, 2048:4096], xnT,
                                        lambda b, t=t: b * 2048 + t * 128,
                                        ve, t_xn2)
        if r < NT:
            t = r
            sl = t % 2
            if t + 1 < NT:
                sl2 = (t + 1) % 2
                for eng, tok in x_free[sl2]:
                    sy.wait(eng, tok)
                u1[t + 1]["x"] = sy.emit(lambda e, t=t, sl2=sl2: e.dma_start(
                    AXS[sl2][:, :], x_d[(t + 1) * 128:(t + 2) * 128, :]), 16)
            sq_bank = 6 if t % 2 == 0 else 7
            t_r1, t_sq1, _ = ln_stats(AXS[sl], sq_bank, 0, 1, [(sy, u1[t]["x"])])
            if t >= 1:
                ve.wait(te, u1[t - 1]["tr1te"])
            u1[t]["xn1"] = ve.emit(lambda e, sl=sl: e.tensor_scalar(
                SLAB[:, 0:2048], AXS[sl][:, :], SM[:, 0:1], SM[:, 1:2],
                ALU.subtract, ALU.mult), 1)

    t_ffn_mm_done = te.n
    wkv_cast = 0
    stg_free2 = [(None, 0), (None, 0)]
    for k in range(NE):
        sl = k % 2
        eng, cnt = stg_free2[sl]
        if eng is not None and cnt:
            sy.wait(eng, cnt)
        t_ld = sy.emit(lambda e, k=k, sl=sl: e.dma_start(
            ASTG[sl][:, :], wkv_d[k * 128:(k + 1) * 128, :]), 16)
        se.wait(sy, t_ld)
        if k == 0:
            se.wait(te, t_ffn_mm_done)
        wkv_cast = se.emit(lambda e, k=k, sl=sl: e.copy(
            QY[:, k * 1024:(k + 1) * 1024], ASTG[sl][:, :]), 1)
        stg_free2[sl] = (se, wkv_cast)

    def rope_evac(psum_ap, dst_ap, n, cs_ap, te_tok):
        ve.wait(te, te_tok)
        ve.emit(lambda e: e.tensor_tensor(AT1[0:64, 0:n], psum_ap[0:64, 0:n],
                                          cs_ap[0:64, 0:n], ALU.mult))
        ve.emit(lambda e: e.tensor_tensor(AT1[64:128, 0:n], psum_ap[64:128, 0:n],
                                          cs_ap[0:64, 0:n], ALU.mult))
        ve.emit(lambda e: e.tensor_tensor(AT2[0:64, 0:n], psum_ap[64:128, 0:n],
                                          cs_ap[64:128, 0:n], ALU.mult))
        ve.emit(lambda e: e.tensor_tensor(AT2[64:128, 0:n], psum_ap[0:64, 0:n],
                                          cs_ap[64:128, 0:n], ALU.mult))
        ve.emit(lambda e: e.tensor_sub(dst_ap[0:64, 0:n], AT1[0:64, 0:n],
                                       AT2[0:64, 0:n]))
        return ve.emit(lambda e: e.tensor_add(dst_ap[64:128, 0:n],
                                              AT1[64:128, 0:n], AT2[64:128, 0:n]), 1)

    t_k_done = 0
    te.wait(ve, t_xnT_done)
    te.wait(se, wkv_cast)
    te.wait(se, t_rows)
    for j in range(NKV):
        kb4 = [(j % 2) * 4 + i for i in range(4)]
        for b in kb4:
            te_wait_free(b)
        for nn in range(4):
            te.emit(lambda e, j=j, nn=nn, kb4=kb4: e.matmul(
                ps[kb4[nn]][:, 0:512], R_bkv[0:1, j * 128:(j + 1) * 128],
                R_ones64[0:1, 0:512], start=True, stop=False))
        t_mm = 0
        for k in range(NE):
            for nn in range(4):
                t_mm = te.emit(lambda e, j=j, nn=nn, kb4=kb4, k=k: e.matmul(
                    ps[kb4[nn]][:, 0:512],
                    QY[:, k * 1024 + j * 128:k * 1024 + (j + 1) * 128],
                    xnT[:, k * 2048 + nn * 512:k * 2048 + (nn + 1) * 512],
                    start=False, stop=(k == NE - 1)),
                    1 if (k == NE - 1 and nn == 3) else 0)
        for nn in range(4):
            t_k_done = rope_evac(ps[kb4[nn]][:, 0:512],
                                 KT[:, j * 2048 + nn * 512:j * 2048 + (nn + 1) * 512],
                                 512, COSIN[:, nn * 512:(nn + 1) * 512], t_mm)
            ps_free[kb4[nn]] = [(ve, t_k_done)]

    t_v_done = 0
    for j in range(NKV):
        vb4 = [(j % 2) * 4 + i for i in range(4)]
        for b in vb4:
            te_wait_free(b)
        for nn in range(4):
            te.emit(lambda e, j=j, nn=nn, vb4=vb4: e.matmul(
                ps[vb4[nn]][:, 0:512], R_bkv[0:1, 512 + j * 128:512 + (j + 1) * 128],
                R_ones64[0:1, 0:512], start=True, stop=False))
        t_mm = 0
        for k in range(NE):
            for nn in range(4):
                t_mm = te.emit(lambda e, j=j, nn=nn, vb4=vb4, k=k: e.matmul(
                    ps[vb4[nn]][:, 0:512],
                    QY[:, k * 1024 + 512 + j * 128:k * 1024 + 512 + (j + 1) * 128],
                    xnT[:, k * 2048 + nn * 512:k * 2048 + (nn + 1) * 512],
                    start=False, stop=(k == NE - 1)),
                    1 if (k == NE - 1 and nn == 3) else 0)
        t_vt = 0
        for nn in range(4):
            ve.wait(te, t_mm)
            t_vt = ve.emit(lambda e, nn=nn, vb4=vb4: e.tensor_copy(
                XNTB[:, nn * 512:(nn + 1) * 512], ps[vb4[nn]][:, 0:512]), 1)
            ps_free[vb4[nn]] = [(ve, t_vt)]
        trb = (4 + (j % 2), 5 - (j % 2))
        te.wait(ve, t_vt)
        tok_half = {}
        for half in range(2):
            b = trb[half]
            te_wait_free(b)
            tt = 0
            for i in range(8):
                c = half * 8 + i
                tt = te.emit(lambda e, b=b, i=i, c=c: e.transpose(
                    psb[b][:, i * 128:(i + 1) * 128],
                    XNTB[:, c * 128:(c + 1) * 128], IDN[:, :]), 1 if i == 7 else 0)
            tok_half[half] = tt
        for half in range(2):
            b = trb[half]
            ve.wait(te, tok_half[half])
            t_v_done = ve.emit(lambda e, b=b, half=half, j=j: e.tensor_copy(
                VT[:, j * 2048 + half * 1024:j * 2048 + (half + 1) * 1024],
                psb[b][:, 0:1024]), 1)
            ps_free[b] = [(ve, t_v_done)]

    wq_free = [(None, 0), (None, 0)]
    stg_free3 = [(se, wkv_cast), (se, wkv_cast)]
    for h in range(NH):
        wsl = h % 2
        eng, cnt = wq_free[wsl]
        if eng is not None and cnt:
            se.wait(eng, cnt)
        t_cast = 0
        for half in range(2):
            ssl = (h * 2 + half) % 2
            eng2, cnt2 = stg_free3[ssl]
            if eng2 is not None and cnt2:
                sy.wait(eng2, cnt2)
            t_ld = 0
            for i in range(8):
                k = half * 8 + i
                t_ld = sy.emit(lambda e, h=h, k=k, i=i, ssl=ssl: e.dma_start(
                    ASTG[ssl][:, i * 128:(i + 1) * 128],
                    wq_d[k * 128:(k + 1) * 128, h * 128:(h + 1) * 128]), 16)
            se.wait(sy, t_ld)
            t_cast = se.emit(lambda e, half=half, ssl=ssl, wsl=wsl: e.activation(
                AWQ[wsl][:, half * 1024:(half + 1) * 1024], ASTG[ssl][:, :],
                AF.Copy, scale=0.08838834764831845), 1)
            stg_free3[ssl] = (se, t_cast)
        t_q_mm = 0
        qb4 = [(h % 2) * 4 + i for i in range(4)]
        te.wait(se, t_cast)
        for b in qb4:
            te_wait_free(b)
        for nn in range(4):
            te.emit(lambda e, h=h, nn=nn, qb4=qb4: e.matmul(
                ps[qb4[nn]][:, 0:512], R_bq[0:1, h * 128:(h + 1) * 128],
                R_ones0[0:1, 0:512], start=True, stop=False))
        for k in range(NE):
            for nn in range(4):
                t_q_mm = te.emit(lambda e, nn=nn, k=k, wsl=wsl, qb4=qb4: e.matmul(
                    ps[qb4[nn]][:, 0:512], AWQ[wsl][:, k * 128:(k + 1) * 128],
                    xnT[:, k * 2048 + nn * 512:k * 2048 + (nn + 1) * 512],
                    start=False, stop=(k == NE - 1)),
                    1 if (k == NE - 1 and nn == 3) else 0)
        for nn in range(4):
            tk = rope_evac(ps[qb4[nn]][:, 0:512],
                           QY[:, h * 2048 + nn * 512:h * 2048 + (nn + 1) * 512],
                           512, COSIN[:, nn * 512:(nn + 1) * 512], t_q_mm)
            ps_free[qb4[nn]] = [(ve, tk)]
        wq_free[wsl] = (te, t_q_mm)

    if stages <= 3:
        gp.wait(ve, ve.n)
        _finish_debug()
        return _assemble()

    t_q_final = ve.n
    se.wait(ve, t_q_final)
    te.wait(ve, t_q_final)
    ASC2 = [AR[:, 0:672], AR[:, 672:1344], AR[:, 1344:2016], AR[:, 2016:2688]]
    AEX2 = [AR[:, 2688:3360], AR[:, 3360:4032], AR[:, 4032:4704], AR[:, 4704:5376]]
    PBUF2 = [SLAB[:, 0:768], SLAB[:, 768:1536]]
    PTB2 = [SLAB[:, 1536:2304], SLAB[:, 2304:3072]]
    iters = [(h, qb) for h in range(NH) for qb in range(NT)]
    N = len(iters)
    st_ = [dict() for _ in range(N)]

    def geom(i):
        h, qb = iters[i]
        j = h // 4
        if qb < 4:
            return h, qb, j, (qb + 1) * 128, qb * 128, 0, False
        return h, qb, j, (656 if qb >= 5 else 640), 512, qb - 4, (qb >= 5)

    def emit_S(i):
        h, qb, j, ext, nmid, klo, meta = geom(i)
        par = i % 2
        bA, bB = par, 2 + par
        te_wait_free(bA)
        te_wait_free(bB)
        qcol = h * 2048 + qb * 128
        t_sA = t_sB = 0
        if qb >= 4:
            t_sB = te.emit(lambda e: e.matmul(
                ps[bB][:, 0:128], QY[:, qcol:qcol + 128],
                KT[:, j * 2048 + klo * 128:j * 2048 + (klo + 1) * 128],
                start=True, stop=True), 1)
            t_sA = te.emit(lambda e: e.matmul(
                ps[bA][:, 0:384], QY[:, qcol:qcol + 128],
                KT[:, j * 2048 + (klo + 1) * 128:j * 2048 + (klo + 4) * 128],
                start=True, stop=True), 1)
        elif nmid > 0:
            t_sA = te.emit(lambda e: e.matmul(
                ps[bA][:, 0:nmid], QY[:, qcol:qcol + 128],
                KT[:, j * 2048:j * 2048 + nmid], start=True, stop=True), 1)
        t_sB = te.emit(lambda e: e.matmul(
            ps[bB][:, 128:256], QY[:, qcol:qcol + 128],
            KT[:, j * 2048 + qb * 128:j * 2048 + (qb + 1) * 128],
            start=True, stop=True), 1)
        if meta:
            t_sA = te.emit(lambda e: e.matmul(
                ps[bA][:, 384:400], QY[:, qcol:qcol + 128],
                KT[:, j * 2048:j * 2048 + 16], start=True, stop=True), 1)
        st_[i]["sA"] = t_sA
        st_[i]["sB"] = t_sB

    def emit_E(i):
        h, qb, j, ext, nmid, klo, meta = geom(i)
        par = i % 2
        par4 = i % 4
        bA, bB = par, 2 + par
        asc = ASC2[par4]
        if i >= 4:
            se.wait(ve, st_[i - 4]["nm"])
            ve.wait(se, st_[i - 4]["exp"])
        ev_se = ev_ve = 0
        if qb >= 4:
            mask_ap = MASKS[:, 256:384] if qb == 4 else MASKS[:, 128:256]
            ve.wait(te, st_[i]["sB"])
            ve.emit(lambda e: e.tensor_add(asc[:, 0:128], ps[bB][:, 0:128], mask_ap))
            ev_ve = ve.emit(lambda e: e.tensor_add(
                asc[:, 512:640], ps[bB][:, 128:256], MASKS[:, 0:128]), 1)
            se.wait(te, st_[i]["sA"])
            ev_se = se.emit(lambda e: e.copy(asc[:, 128:512], ps[bA][:, 0:384]), 1)
            if meta:
                ev_se = se.emit(lambda e: e.copy(asc[:, 640:656], ps[bA][:, 384:400]), 1)
            ps_free[bA] = [(se, ev_se)]
            ps_free[bB] = [(ve, ev_ve)]
        else:
            if nmid > 0:
                se.wait(te, st_[i]["sA"])
                ev_se = se.emit(lambda e: e.copy(asc[:, 0:nmid], ps[bA][:, 0:nmid]), 1)
                ps_free[bA] = [(se, ev_se)]
            ve.wait(te, st_[i]["sB"])
            ev_ve = ve.emit(lambda e: e.tensor_add(
                asc[:, nmid:nmid + 128], ps[bB][:, 128:256], MASKS[:, 0:128]), 1)
            ps_free[bB] = [(ve, ev_ve)]
        if ev_se:
            ve.wait(se, ev_se)
        smc = 4 + par4 * 3
        t_nm = ve.emit(lambda e: e.reduce_max(
            SM[:, smc:smc + 1], asc[:, 0:ext], axis=AX.X, negate=True), 1)
        st_[i]["nm"] = t_nm

    def emit_X(i):
        h, qb, j, ext, nmid, klo, meta = geom(i)
        par4 = i % 4
        smc = 4 + par4 * 3
        se.wait(ve, st_[i]["nm"])
        if i >= 4:
            se.wait(ve, st_[i - 4]["div"])
        t_exp = se.emit(lambda e: e.activation(
            AEX2[par4][:, 0:ext], ASC2[par4][:, 0:ext], AF.Exp,
            bias=SM[:, smc:smc + 1], accum_out=SM[:, smc + 1:smc + 2]), 1)
        st_[i]["exp"] = t_exp

    def emit_T(i):
        h, qb, j, ext, nmid, klo, meta = geom(i)
        par = i % 2
        par4 = i % 4
        smc = 4 + par4 * 3
        bT = 4 + par
        bY = 6 + ((qb // 4) % 2)
        nblk = ext // 128
        ve.wait(se, st_[i]["exp"])
        t_rc = ve.emit(lambda e: e.reciprocal(
            SM[:, smc + 2:smc + 3], SM[:, smc + 1:smc + 2]), 1)
        ve.wait(ve, t_rc)
        if i >= 2:
            ve.wait(te, st_[i - 2]["pt"])
        t_div = ve.emit(lambda e: e.tensor_scalar(
            PBUF2[par][:, 0:ext], AEX2[par4][:, 0:ext], SM[:, smc + 2:smc + 3],
            None, ALU.mult), 1)
        st_[i]["div"] = t_div
        te.wait(ve, t_div)
        te_wait_free(bT)
        t_tr = 0
        for b_ in range(nblk):
            t_tr = te.emit(lambda e, b_=b_: e.transpose(
                psb[bT][:, b_ * 128:(b_ + 1) * 128],
                PBUF2[par][:, b_ * 128:(b_ + 1) * 128], IDN[:, :]),
                1 if (b_ == nblk - 1 and not meta) else 0)
        if meta:
            t_tr = te.emit(lambda e: e.transpose(
                psb[bT][0:16, 640:768], PBUF2[par][:, 640:656], IDN[:, :]), 1)
        st_[i]["pt"] = t_tr
        ve.wait(te, t_tr)
        if i >= 2:
            ve.wait(te, st_[i - 2]["pv"])
        w = nblk * 128 + (128 if meta else 0)
        t_ptc = ve.emit(lambda e: e.tensor_copy(PTB2[par][:, 0:w],
                                                psb[bT][:, 0:w]), 1)
        ps_free[bT] = [(ve, t_ptc)]
        te.wait(ve, t_ptc)
        if qb % 4 == 0:
            te_wait_free(bY)
        yoff = (qb % 4) * 128
        t_pv = 0
        for b_ in range(nblk):
            ktile = klo + b_ if qb >= 4 else b_
            t_pv = te.emit(lambda e, b_=b_, ktile=ktile: e.matmul(
                ps[bY][:, yoff:yoff + 128],
                VT[:, j * 2048 + ktile * 128:j * 2048 + (ktile + 1) * 128],
                PTB2[par][:, b_ * 128:(b_ + 1) * 128],
                start=(b_ == 0), stop=(b_ == nblk - 1 and not meta)),
                1 if (b_ == nblk - 1 and not meta) else 0)
        if meta:
            t_pv = te.emit(lambda e: e.matmul(
                ps[bY][:, yoff:yoff + 128], VT[0:16, j * 2048:j * 2048 + 128],
                PTB2[par][0:16, nblk * 128:nblk * 128 + 128],
                start=False, stop=True), 1)
        st_[i]["pv"] = t_pv
        if qb % 4 == 3:
            ve.wait(te, t_pv)
            tk = ve.emit(lambda e: e.tensor_copy(
                QY[:, h * 2048 + (qb - 3) * 128:h * 2048 + (qb + 1) * 128],
                ps[bY][:, 0:512]), 1)
            ps_free[bY] = [(ve, tk)]
            st_[i]["ye"] = tk
        return st_[i].get("ye", 0)

    t_y_done = 0
    for r in range(N + 3):
        if r < N:
            emit_S(r)
        if 0 <= r - 1 < N:
            emit_E(r - 1)
        if 0 <= r - 2 < N:
            emit_X(r - 2)
        if 0 <= r - 3 < N:
            tk = emit_T(r - 3)
            if tk:
                t_y_done = tk

    for a_ in (te, ve, se, gp, sy):
        for b_ in (te, ve, se, gp, sy):
            if a_ is not b_:
                a_.wait(b_, b_.n)
    t_msb = se.emit(lambda e: e.copy(MSB[:, :], MS[:, :]), 1)
    te.wait(se, t_msb)
    te_wait_free(4)
    t_mst = te.emit(lambda e: e.transpose(psb[4][0:32, 0:128], MSB[:, :],
                                          IDN[:, :]), 1)
    ve.wait(te, t_mst)
    t_mstc = ve.emit(lambda e: e.tensor_copy(MST[:, :], psb[4][0:32, 0:128]), 1)
    ps_free[4] = [(ve, t_mstc)]
    sy.wait(ve, t_mstc)
    t_msr = 0
    for t in range(NT):
        sy.emit(lambda e, t=t: e.dma_start(R_m[0:1, t * 128:(t + 1) * 128],
                                           MST[t:t + 1, :]), 16)
        t_msr = sy.emit(lambda e, t=t: e.dma_start(
            R_sd[0:1, t * 128:(t + 1) * 128], MST[16 + t:17 + t, :]), 16)
    te.wait(sy, t_msr)
    sd_toks = []
    for i in range(4):
        b = 4 + i
        te_wait_free(b)
        sd_toks.append(te.emit(lambda e, b=b, i=i: e.matmul(
            ps[b][:, 0:512], R_ones0[0:1, 0:128],
            R_sd[0:1, i * 512:(i + 1) * 512], start=True, stop=True), 1))
    t_sdr = 0
    for i in range(4):
        b = 4 + i
        ve.wait(te, sd_toks[i])
        t_sdr = ve.emit(lambda e, b=b, i=i: e.tensor_copy(
            ASDR[:, i * 512:(i + 1) * 512], ps[b][:, 0:512]), 1)
        ps_free[b] = [(ve, t_sdr)]

    te.wait(ve, t_y_done)
    if stages <= 4.2:
        gp.wait(ve, t_sdr)
        _finish_debug()
        return _assemble()

    wo_free = [(None, 0), (None, 0)]
    t_x2_done = 0
    for f in range(int(NE) if stages > 4.5 else 1):
        wsl = f % 2
        eng, cnt = wo_free[wsl]
        if eng is not None and cnt:
            sy.wait(eng, cnt)
        t_ld = 0
        for k in range(NE):
            t_ld = sy.emit(lambda e, f=f, k=k, wsl=wsl: e.dma_start(
                AXS[wsl][:, k * 128:(k + 1) * 128],
                wo_d[k * 128:(k + 1) * 128, f * 128:(f + 1) * 128]), 16)
        se.wait(sy, t_ld)
        t_woc = se.emit(lambda e, wsl=wsl: e.copy(AWO[wsl][:, :],
                                                  AXS[wsl][:, 0:2048]), 1)
        wo_free[wsl] = (se, t_woc)
        g = (f % 2) * 4
        te.wait(se, t_woc)
        t_wo_mm = 0
        gb4 = [g + i for i in range(4)]
        for b in gb4:
            te_wait_free(b)
        for nn in range(4):
            te.emit(lambda e, f=f, nn=nn, gb4=gb4: e.matmul(
                ps[gb4[nn]][:, 0:512], R_bo[0:1, f * 128:(f + 1) * 128],
                R_ones0[0:1, 0:512], start=True, stop=False))
            te.emit(lambda e, nn=nn, gb4=gb4: e.matmul(
                ps[gb4[nn]][:, 0:512], R_ones0[0:1, 0:128],
                R_m[0:1, nn * 512:(nn + 1) * 512], start=False, stop=False))
        for k in range(NE):
            for nn in range(4):
                t_wo_mm = te.emit(lambda e, nn=nn, gb4=gb4, k=k, wsl=wsl: e.matmul(
                    ps[gb4[nn]][:, 0:512], AWO[wsl][:, k * 128:(k + 1) * 128],
                    QY[:, k * 2048 + nn * 512:k * 2048 + (nn + 1) * 512],
                    start=False, stop=(k == NE - 1)),
                    1 if (k == NE - 1 and nn == 3) else 0)
        mm = {nn: t_wo_mm for nn in range(4)}
        wo_free[wsl] = (te, t_wo_mm)
        for nn in range(4):
            bank = g + nn
            ve.wait(te, mm[nn])
            if f == 0 and nn == 0:
                ve.wait(ve, t_sdr)
            cl, ch = f * 2048 + nn * 512, f * 2048 + (nn + 1) * 512
            ve.emit(lambda e, cl=cl, ch=ch, nn=nn: e.tensor_tensor(
                SLAB[:, 0:512], xnT[:, cl:ch],
                ASDR[:, nn * 512:(nn + 1) * 512], ALU.mult))
            t_x2_done = ve.emit(lambda e, bank=bank, cl=cl, ch=ch: e.tensor_add(
                xnT[:, cl:ch], ps[bank][:, 0:512], SLAB[:, 0:512]), 1)
            ps_free[bank] = [(ve, t_x2_done)]

    if stages <= 5:
        gp.wait(ve, t_x2_done)
        _finish_debug()
        return _assemble()

    t_u4_cast = wo_free[0][1] if wo_free[0][0] is not None else 0
    t_u4_cast2 = wo_free[1][1] if wo_free[1][0] is not None else 0
    t_u4_sy = sy.n
    t_srep2 = build_srep(ve, t_x2_done)
    w2_tok = load_wffn((te, te.n))
    out_free = [[(sy, t_u4_sy), (se, se.n)], [(sy, t_u4_sy), (se, se.n)]]
    st_tok = 0
    for t in range(NT):
        sl = t % 2
        tok_half = {}
        for half in range(2):
            b = 4 + half
            te_wait_free(b)
            te.wait(ve, t_x2_done)
            t_tr = 0
            for i in range(8):
                f = half * 8 + i
                t_tr = te.emit(lambda e, b=b, i=i, f=f, t=t: e.transpose(
                    psb[b][:, i * 128:(i + 1) * 128],
                    xnT[:, f * 2048 + t * 128:f * 2048 + (t + 1) * 128],
                    IDN[:, :]), 1 if i == 7 else 0)
            tok_half[half] = t_tr
        t_x2t = 0
        for half in range(2):
            b = 4 + half
            ve.wait(te, tok_half[half])
            t_x2t = ve.emit(lambda e, b=b, half=half: e.tensor_copy(
                SLAB[:, half * 1024:(half + 1) * 1024], psb[b][:, 0:1024]), 1)
            ps_free[b] = [(ve, t_x2t)]
        sq_bank = 6 if t % 2 == 0 else 7
        t_r3, t_sq3, _ = ln_stats(SLAB[:, 0:2048], sq_bank, 0, 1, [(ve, t_x2t)])
        t_xn3 = ve.emit(lambda e: e.tensor_scalar(
            SLAB[:, 2048:4096], SLAB[:, 0:2048], SM[:, 0:1], SM[:, 1:2],
            ALU.subtract, ALU.mult), 1)
        t_tr3 = transpose_tile(SLAB[:, 2048:4096], XNTB, lambda b: b * 128,
                               ve, t_xn3)
        for eng, tok in out_free[sl]:
            ve.wait(eng, tok)
        t_o = ffn_tile(t, None, AXS[sl], SLAB[:, 0:2048], w2_tok, t_tr3)
        gp.wait(ve, t_o)
        st_tok = gp.emit(lambda e, t=t, sl=sl: e.dma_start(
            out_d[t * 128:(t + 1) * 128, :], AXS[sl][:, :]), 16)
        out_free[sl] = [(gp, st_tok)]

    gp.wait(gp, st_tok)
    return _assemble()


def _host_prep(inputs):
    perm = np.concatenate([np.arange(0, 128, 2), np.arange(1, 128, 2)])
    permE = np.concatenate([h * 128 + perm for h in range(16)])
    perm512 = np.concatenate([j * 128 + perm for j in range(4)])

    wq = np.ascontiguousarray(inputs["L0_attn_Wq"][:, permE])
    wkv = inputs["L0_attn_Wkv"].copy()
    wkv[:, 0:512] = wkv[:, perm512]
    wkv = np.ascontiguousarray(wkv)
    bq = np.ascontiguousarray(inputs["L0_attn_bq"][permE]).reshape(1, -1)
    bkv = inputs["L0_attn_bkv"].copy()
    bkv[0:512] = bkv[perm512]
    bkv = bkv.reshape(1, -1)
    wffnT = np.ascontiguousarray(inputs["L0_ffn_weight"].T)
    wo = np.ascontiguousarray(inputs["L0_attn_Wo"])

    inv = (1.0 / 10000.0 ** (np.arange(0, 128, 2, dtype=np.float64) / 128.0))
    ang = inv[:, None] * np.arange(S, dtype=np.float64)[None, :]
    cosin = np.concatenate([np.cos(ang), np.sin(ang)], axis=0).astype(np.float32)

    q = np.arange(128)[:, None]
    k = np.arange(128)[None, :]
    mL = np.where(k <= q, 0.0, NEG).astype(np.float32)
    mU = np.where(k >= q + 1, 0.0, NEG).astype(np.float32)
    mMU = np.where((k >= q + 1) | (k < 16), 0.0, NEG).astype(np.float32)
    masks = np.concatenate([mL, mU, mMU], axis=1).astype(ml_dtypes.bfloat16)

    idn = np.eye(128, dtype=ml_dtypes.bfloat16)

    shared = {
        "wffnT": wffnT, "wq": wq, "wkv": wkv, "wo": wo,
        "fscale": inputs["L0_ffn_scale_weight"].reshape(1, -1).astype(np.float32),
        "fbias": inputs["L0_ffn_bias"].reshape(1, -1).astype(np.float32),
        "bq": bq.astype(np.float32), "bkv": bkv.astype(np.float32),
        "bo": inputs["L0_attn_bo"].reshape(1, -1).astype(np.float32),
        "cosin": cosin, "masks": masks, "idn": idn,
    }
    return shared


def kernel(**inputs):
    shared = _host_prep(inputs)
    x = np.ascontiguousarray(inputs["L0_attn_x"], dtype=np.float32)
    nc = build_program()
    in_maps = [dict(shared, x=np.ascontiguousarray(x[b])) for b in range(8)]
    res = run_bass_kernel_spmd(nc, in_maps, core_ids=list(range(8)))
    out = np.stack([np.asarray(r["out"]) for r in res.results], axis=0)
    return out.astype(np.float32)


# version 09
# speedup vs baseline: 1.0015x; 1.0015x over previous
import numpy as np
import ml_dtypes
import concourse.bass as bass
import concourse.mybir as mybir
from concourse.bass_utils import run_bass_kernel_spmd

F32 = mybir.dt.float32
BF16 = mybir.dt.bfloat16
AF = mybir.ActivationFunctionType
ALU = mybir.AluOpType
AX = mybir.AxisListType

S = 2048
E = 2048
NT = S // 128
NE = E // 128
NH = 16
NKV = 4
HD = 128
NEG = -1e30


class Eng:
    def __init__(self, nc, name):
        self.sem = nc.alloc_semaphore(name + "_sem")
        self.n = 0
        self.ops = []

    def emit(self, fn, inc=0):
        self.ops.append((fn, inc, self.sem))
        self.n += inc
        return self.n

    def wait(self, other, val):
        assert val <= other.n, f"wait on future count {val} > {other.n}"
        sem = other.sem
        self.ops.append((lambda e, s=sem, v=val: e.wait_ge(s, v), 0, None))

    def replay(self, e):
        for fn, inc, sem in self.ops:
            inst = fn(e)
            if inc:
                inst.then_inc(sem, inc)


def build_program(stages=6, dump=None):
    nc = bass.Bass("TRN2", target_bir_lowering=False,
                   dynamic_dma_scratch_size=2048)

    x_d = nc.dram_tensor("x", [S, E], F32, kind="ExternalInput").ap()
    wffnT_d = nc.dram_tensor("wffnT", [E, E], F32, kind="ExternalInput").ap()
    wq_d = nc.dram_tensor("wq", [E, E], F32, kind="ExternalInput").ap()
    wkv_d = nc.dram_tensor("wkv", [E, 1024], F32, kind="ExternalInput").ap()
    wo_d = nc.dram_tensor("wo", [E, E], F32, kind="ExternalInput").ap()
    scale_d = nc.dram_tensor("fscale", [1, E], F32, kind="ExternalInput").ap()
    fbias_d = nc.dram_tensor("fbias", [1, E], F32, kind="ExternalInput").ap()
    bq_d = nc.dram_tensor("bq", [1, E], F32, kind="ExternalInput").ap()
    bkv_d = nc.dram_tensor("bkv", [1, 1024], F32, kind="ExternalInput").ap()
    bo_d = nc.dram_tensor("bo", [1, E], F32, kind="ExternalInput").ap()
    cosin_d = nc.dram_tensor("cosin", [128, S], F32, kind="ExternalInput").ap()
    masks_d = nc.dram_tensor("masks", [128, 384], BF16, kind="ExternalInput").ap()
    idn_d = nc.dram_tensor("idn", [128, 128], BF16, kind="ExternalInput").ap()
    out_d = nc.dram_tensor("out", [S, E], F32, kind="ExternalOutput").ap()

    xnT = nc.alloc_sbuf_tensor("xnT", [128, NE * S], BF16)
    QY = nc.alloc_sbuf_tensor("QY", [128, NH * S], BF16)
    KT = nc.alloc_sbuf_tensor("KT", [128, NKV * S], BF16)
    VT = nc.alloc_sbuf_tensor("VT", [128, NKV * S], BF16)
    AR = nc.alloc_sbuf_tensor("AR", [128, 7168], F32)
    COSIN = nc.alloc_sbuf_tensor("COSIN", [128, S], F32)
    ROWS = nc.alloc_sbuf_tensor("ROWS", [128, 6144], BF16)
    MST = nc.alloc_sbuf_tensor("MST", [32, 128], BF16)
    SLAB = nc.alloc_sbuf_tensor("SLAB", [128, 4096], BF16)
    XNTB = nc.alloc_sbuf_tensor("XNTB", [128, 2048], BF16)
    MASKS = nc.alloc_sbuf_tensor("MASKS", [128, 384], BF16)
    IDN = nc.alloc_sbuf_tensor("IDN", [128, 128], BF16)
    MS = nc.alloc_sbuf_tensor("MS", [128, 32], F32)
    MSB = nc.alloc_sbuf_tensor("MSB", [128, 32], BF16)
    SM = nc.alloc_sbuf_tensor("SM", [128, 16], F32)
    CONST = nc.alloc_sbuf_tensor("CONST", [128, 4], F32)

    ps = [nc.alloc_psum_tensor(f"ps{i}", [128, 512], F32) for i in range(8)]
    psb = [p[:, :].bitcast(BF16) for p in ps]

    AXS = [AR[:, 0:2048], AR[:, 2048:4096]]
    ASTG = [AR[:, 4096:5120], AR[:, 5120:6144]]
    ASREP = AR[:, 6144:7168].bitcast(BF16)
    ASDR = AR[:, 6144:7168].bitcast(BF16)
    AT1 = AR[:, 0:512]
    AT2 = AR[:, 512:1024]
    AWQ = [AR[:, 1024:2048].bitcast(BF16), AR[:, 2048:3072].bitcast(BF16)]
    AWO = [AR[:, 4096:5120].bitcast(BF16), AR[:, 5120:6144].bitcast(BF16)]
    ASC = AR[:, 0:672]
    AEX = AR[:, 1024:1696]
    PBUF = SLAB[:, 0:768]
    PTB = SLAB[:, 768:1536]
    R_bq = ROWS[0:1, 0:2048]
    R_bo = ROWS[0:1, 2048:4096]
    R_ones0 = ROWS[0:1, 4096:4608]
    R_m = KT[0:1, 0:2048]
    R_sd = KT[0:1, 2048:4096]
    R_bkv = ROWS[64:65, 0:1024]
    R_scale = ROWS[64:65, 1024:3072]
    R_bias = ROWS[64:65, 3072:5120]
    R_ones64 = ROWS[64:65, 5120:5632]

    sy = Eng(nc, "sy")
    gp = Eng(nc, "gp")
    te = Eng(nc, "te")
    ve = Eng(nc, "ve")
    se = Eng(nc, "se")

    ps_free = [[] for _ in range(8)]

    def te_wait_free(bank):
        for eng, cnt in ps_free[bank]:
            if cnt > 0:
                te.wait(eng, cnt)

    sy.emit(lambda e: e.dma_start(COSIN[:, :], cosin_d[:, :]), 16)
    sy.emit(lambda e: e.dma_start(MASKS[:, :], masks_d[:, :]), 16)
    t_pro = sy.emit(lambda e: e.dma_start(IDN[:, :], idn_d[:, :]), 16)
    sy.emit(lambda e: e.dma_start(AR[0:1, 0:2048], scale_d[0:1, :]), 16)
    sy.emit(lambda e: e.dma_start(AR[0:1, 2048:4096], fbias_d[0:1, :]), 16)
    t_w1 = sy.emit(lambda e: e.dma_start(AR[0:1, 4096:6144], bq_d[0:1, :]), 16)
    ve.emit(lambda e: e.memset(R_ones0, 1.0))
    ve.emit(lambda e: e.memset(R_ones64, 1.0))
    ve.emit(lambda e: e.memset(CONST[:, 0:1], 1e-5))
    t_const = ve.emit(lambda e: e.memset(CONST[:, 1:2], 1.0 / E), 1)
    se.wait(sy, t_w1)
    se.emit(lambda e: e.activation(R_scale, AR[0:1, 0:2048], AF.Copy, scale=0.5))
    se.emit(lambda e: e.activation(R_bias, AR[0:1, 2048:4096], AF.Copy, scale=0.5))
    t_w1c = se.emit(lambda e: e.activation(R_bq, AR[0:1, 4096:6144], AF.Copy, scale=0.08838834764831845), 1)
    sy.wait(se, t_w1c)
    sy.emit(lambda e: e.dma_start(AR[0:1, 0:1024], bkv_d[0:1, :]), 16)
    t_w2 = sy.emit(lambda e: e.dma_start(AR[0:1, 1024:3072], bo_d[0:1, :]), 16)
    se.wait(sy, t_w2)
    se.emit(lambda e: e.copy(R_bkv, AR[0:1, 0:1024]))
    t_rows = se.emit(lambda e: e.copy(R_bo, AR[0:1, 1024:3072]), 1)

    def build_srep(dep_eng, dep_tok):
        te.wait(se, t_rows)
        if dep_eng is not None:
            te.wait(dep_eng, dep_tok)
        toks = []
        for i in range(4):
            b = 4 + i
            te_wait_free(b)
            toks.append(te.emit(lambda e, b=b, i=i: e.matmul(
                ps[b][:, 0:512], R_ones64[0:1, 0:128],
                R_scale[0:1, i * 512:(i + 1) * 512], start=True, stop=True), 1))
        last = 0
        for i in range(4):
            b = 4 + i
            ve.wait(te, toks[i])
            last = ve.emit(lambda e, b=b, i=i: e.tensor_copy(
                ASREP[:, i * 512:(i + 1) * 512], ps[b][:, 0:512]), 1)
            ps_free[b] = [(ve, last)]
        return last

    ln_war = {"ve": 0}

    def ln_stats(src_ap, sq_bank, mcol, rcol, deps):
        for eng, tok in deps:
            if tok:
                se.wait(eng, tok)
        if ln_war["ve"]:
            se.wait(ve, ln_war["ve"])
        for eng_f, cnt_f in ps_free[sq_bank]:
            if cnt_f:
                se.wait(eng_f, cnt_f)
        t_sq = 0
        for c in range(4):
            t_sq = se.emit(lambda e, c=c: e.activation(
                ps[sq_bank][:, 0:512], src_ap[:, c * 512:(c + 1) * 512],
                AF.Square, accum_out=SM[:, 8 + c:9 + c]), 1 if c == 3 else 0)
        ps_free[sq_bank] = [(se, t_sq)]
        for eng, tok in deps:
            if tok:
                ve.wait(eng, tok)
        t1 = ve.emit(lambda e: e.reduce_sum(SM[:, 12:13], src_ap[:, :], axis=AX.X), 1)
        ve.wait(ve, t1)
        t2 = ve.emit(lambda e: e.tensor_scalar(SM[:, mcol:mcol + 1], SM[:, 12:13],
                                               CONST[:, 1:2], None, ALU.mult), 1)
        ve.wait(se, t_sq)
        t3 = ve.emit(lambda e: e.reduce_sum(SM[:, 13:14], SM[:, 8:12], axis=AX.X), 1)
        ve.wait(ve, t3)
        t4 = ve.emit(lambda e: e.tensor_scalar(SM[:, 13:14], SM[:, 13:14],
                                               CONST[:, 1:2], None, ALU.mult), 1)
        ve.wait(ve, t2)
        ve.wait(ve, t4)
        t5 = ve.emit(lambda e: e.tensor_tensor(SM[:, 14:15], SM[:, mcol:mcol + 1],
                                               SM[:, mcol:mcol + 1], ALU.mult), 1)
        ve.wait(ve, t5)
        t_var = ve.emit(lambda e: e.tensor_sub(SM[:, 13:14], SM[:, 13:14],
                                               SM[:, 14:15]), 1)
        ln_war["ve"] = t_var
        se.wait(ve, t_var)
        t_sd = se.emit(lambda e: e.activation(SM[:, 15:16], SM[:, 13:14],
                                              AF.Sqrt, bias=CONST[:, 0:1]), 1)
        ve.wait(se, t_sd)
        t_r = ve.emit(lambda e: e.reciprocal(SM[:, rcol:rcol + 1], SM[:, 15:16]), 1)
        ve.wait(ve, t_r)
        return t_r, t_sq, t_sd

    def transpose_tile(src_ap, dst, dst_col, dep_eng, dep_tok, banks=(4, 5)):
        te.wait(dep_eng, dep_tok)
        tok_half = {}
        for half in range(2):
            b = banks[half]
            te_wait_free(b)
            t = 0
            for i in range(8):
                blk = half * 8 + i
                t = te.emit(lambda e, b=b, i=i, blk=blk: e.transpose(
                    psb[b][:, i * 128:(i + 1) * 128],
                    src_ap[:, blk * 128:(blk + 1) * 128], IDN[:, :]),
                    1 if i == 7 else 0)
            tok_half[half] = t
        last = 0
        for half in range(2):
            b = banks[half]
            ve.wait(te, tok_half[half])
            for i in range(8):
                blk = half * 8 + i
                last = ve.emit(lambda e, b=b, i=i, blk=blk: e.tensor_copy(
                    dst[:, dst_col(blk):dst_col(blk) + 128],
                    psb[b][:, i * 128:(i + 1) * 128]), 1 if i == 7 else 0)
            ps_free[b] = [(ve, last)]
        return last

    def load_wffn(extra_dep):
        stg_free = [extra_dep, extra_dep]
        last = 0
        for k in range(NE):
            for h2 in range(2):
                sl = (k * 2 + h2) % 2
                eng, cnt = stg_free[sl]
                if eng is not None and cnt:
                    sy.wait(eng, cnt)
                t_ld = sy.emit(lambda e, k=k, h2=h2, sl=sl: e.dma_start(
                    ASTG[sl][:, :], wffnT_d[k * 128:(k + 1) * 128,
                                            h2 * 1024:(h2 + 1) * 1024]), 16)
                ve.wait(sy, t_ld)
                last = ve.emit(lambda e, k=k, h2=h2, sl=sl: e.tensor_tensor(
                    QY[:, k * 2048 + h2 * 1024:k * 2048 + (h2 + 1) * 1024],
                    ASTG[sl][:, :], ASREP[:, h2 * 1024:(h2 + 1) * 1024],
                    ALU.mult), 1)
                stg_free[sl] = (ve, last)
        return last

    def ffn_tile(t, x_ap, out_ap, out_in1_ap, w_tok, xnt_tok):
        pair = (0, 1) if t % 2 == 0 else (2, 3)
        te.wait(ve, w_tok)
        te.wait(ve, xnt_tok)
        te.wait(se, t_rows)
        banks = [pair[0], pair[0], pair[1], pair[1]]
        offs = [0, 0, 0, 0]
        quad = (0, 1, 2, 3)
        for b in quad:
            te_wait_free(b)
        for nn in range(4):
            te.emit(lambda e, nn=nn: e.matmul(
                ps[quad[nn]][:, 0:512], R_ones64[0:1, 0:128],
                R_bias[0:1, nn * 512:(nn + 1) * 512], start=True, stop=False))
        t_mm = 0
        for k in range(NE):
            for nn in range(4):
                t_mm = te.emit(lambda e, nn=nn, k=k: e.matmul(
                    ps[quad[nn]][:, 0:512], XNTB[:, k * 128:(k + 1) * 128],
                    QY[:, k * 2048 + nn * 512:k * 2048 + nn * 512 + 512],
                    start=False, stop=(k == NE - 1)),
                    1 if (k == NE - 1 and nn == 3) else 0)
        last = 0
        for nn in range(4):
            ve.wait(te, t_mm)
            last = ve.emit(lambda e, nn=nn: e.tensor_add(
                out_ap[:, nn * 512:(nn + 1) * 512], ps[quad[nn]][:, 0:512],
                out_in1_ap[:, nn * 512:(nn + 1) * 512]), 1)
            ps_free[quad[nn]] = [(ve, last)]
        return last

    def _finish_debug():
        if dump is not None:
            se.wait(ve, ve.n)
            se.wait(te, te.n)
            se.wait(sy, sy.n)
            src, ncols = dump_map[dump]
            ntiles = ncols // 2048
            tok = se.n
            dtok = 0
            for i in range(ntiles):
                se.wait(gp, dtok)
                tok = se.emit(lambda e, i=i, src=src: e.copy(
                    AXS[0][:, :], src[:, i * 2048:(i + 1) * 2048]), 1)
                gp.wait(se, tok)
                dtok = gp.emit(lambda e, i=i: e.dma_start(
                    out_d[i * 128:(i + 1) * 128, :], AXS[0][:, :]), 16)
            gp.wait(gp, dtok)
            return
        t_end = gp.emit(lambda e: e.dma_start(out_d[0:128, :], AR[:, 0:2048]), 16)
        gp.wait(gp, t_end)

    def _assemble():
        with nc.Block() as block:
            @block.sync
            def _(e):
                sy.replay(e)

            @block.tensor
            def _(e):
                te.replay(e)

            @block.vector
            def _(e):
                ve.replay(e)

            @block.scalar
            def _(e):
                se.replay(e)

            @block.gpsimd
            def _(e):
                gp.replay(e)
        return nc

    dump_map = {"xnT": (xnT, NE * S), "KT": (KT, NKV * S), "VT": (VT, NKV * S),
                "QY": (QY, NH * S)}
    if stages <= 0:
        gp.wait(se, t_rows)
        _finish_debug()
        return _assemble()

    te.wait(sy, t_pro)
    ve.wait(sy, t_pro)
    se.wait(ve, t_const)
    gp.wait(sy, t_pro)
    t_srep = build_srep(None, 0)
    w1_tok = load_wffn((se, t_rows))

    x_free = [[(se, t_rows)], [(se, t_rows)]]
    u1 = [dict() for _ in range(NT)]
    for eng, tok in x_free[0]:
        sy.wait(eng, tok)
    u1[0]["x"] = sy.emit(lambda e: e.dma_start(AXS[0][:, :], x_d[0:128, :]), 16)
    t_xnT_done = 0
    for r in range(NT + 1):
        if r >= 1:
            t = r - 1
            sl = t % 2
            t_tr1 = transpose_tile(SLAB[:, 0:2048], XNTB, lambda b: b * 128,
                                   ve, u1[t]["xn1"])
            u1[t]["tr1te"] = te.n
            t_x1 = ffn_tile(t, AXS[sl], AXS[sl], AXS[sl], w1_tok, t_tr1)
            sq_bank = 6 if t % 2 == 0 else 7
            t_r2, t_sq2, t_sd2 = ln_stats(AXS[sl], sq_bank, 2, 3, [(ve, t_x1)])
            ve.emit(lambda e, t=t: e.tensor_copy(MS[:, t:t + 1], SM[:, 2:3]))
            ve.emit(lambda e, t=t: e.tensor_copy(MS[:, 16 + t:17 + t],
                                                 SM[:, 15:16]), 1)
            t_xn2 = ve.emit(lambda e, sl=sl: e.tensor_scalar(
                SLAB[:, 2048:4096], AXS[sl][:, :], SM[:, 2:3], SM[:, 3:4],
                ALU.subtract, ALU.mult), 1)
            x_free[sl] = [(ve, t_xn2), (se, t_sq2)]
            t_xnT_done = transpose_tile(SLAB[:, 2048:4096], xnT,
                                        lambda b, t=t: b * 2048 + t * 128,
                                        ve, t_xn2)
        if r < NT:
            t = r
            sl = t % 2
            if t + 1 < NT:
                sl2 = (t + 1) % 2
                for eng, tok in x_free[sl2]:
                    sy.wait(eng, tok)
                u1[t + 1]["x"] = sy.emit(lambda e, t=t, sl2=sl2: e.dma_start(
                    AXS[sl2][:, :], x_d[(t + 1) * 128:(t + 2) * 128, :]), 16)
            sq_bank = 6 if t % 2 == 0 else 7
            t_r1, t_sq1, _ = ln_stats(AXS[sl], sq_bank, 0, 1, [(sy, u1[t]["x"])])
            if t >= 1:
                ve.wait(te, u1[t - 1]["tr1te"])
            u1[t]["xn1"] = ve.emit(lambda e, sl=sl: e.tensor_scalar(
                SLAB[:, 0:2048], AXS[sl][:, :], SM[:, 0:1], SM[:, 1:2],
                ALU.subtract, ALU.mult), 1)

    t_ffn_mm_done = te.n
    wkv_cast = 0
    stg_free2 = [(None, 0), (None, 0)]
    for k in range(NE):
        sl = k % 2
        eng, cnt = stg_free2[sl]
        if eng is not None and cnt:
            sy.wait(eng, cnt)
        t_ld = sy.emit(lambda e, k=k, sl=sl: e.dma_start(
            ASTG[sl][:, :], wkv_d[k * 128:(k + 1) * 128, :]), 16)
        se.wait(sy, t_ld)
        if k == 0:
            se.wait(te, t_ffn_mm_done)
        wkv_cast = se.emit(lambda e, k=k, sl=sl: e.copy(
            QY[:, k * 1024:(k + 1) * 1024], ASTG[sl][:, :]), 1)
        stg_free2[sl] = (se, wkv_cast)

    def rope_evac(psum_ap, dst_ap, n, cs_ap, te_tok):
        ve.wait(te, te_tok)
        ve.emit(lambda e: e.tensor_tensor(AT1[0:64, 0:n], psum_ap[0:64, 0:n],
                                          cs_ap[0:64, 0:n], ALU.mult))
        ve.emit(lambda e: e.tensor_tensor(AT1[64:128, 0:n], psum_ap[64:128, 0:n],
                                          cs_ap[0:64, 0:n], ALU.mult))
        ve.emit(lambda e: e.tensor_tensor(AT2[0:64, 0:n], psum_ap[64:128, 0:n],
                                          cs_ap[64:128, 0:n], ALU.mult))
        ve.emit(lambda e: e.tensor_tensor(AT2[64:128, 0:n], psum_ap[0:64, 0:n],
                                          cs_ap[64:128, 0:n], ALU.mult))
        ve.emit(lambda e: e.tensor_sub(dst_ap[0:64, 0:n], AT1[0:64, 0:n],
                                       AT2[0:64, 0:n]))
        return ve.emit(lambda e: e.tensor_add(dst_ap[64:128, 0:n],
                                              AT1[64:128, 0:n], AT2[64:128, 0:n]), 1)

    t_k_done = 0
    te.wait(ve, t_xnT_done)
    te.wait(se, wkv_cast)
    te.wait(se, t_rows)
    for j in range(NKV):
        kb4 = [(j % 2) * 4 + i for i in range(4)]
        for b in kb4:
            te_wait_free(b)
        for nn in range(4):
            te.emit(lambda e, j=j, nn=nn, kb4=kb4: e.matmul(
                ps[kb4[nn]][:, 0:512], R_bkv[0:1, j * 128:(j + 1) * 128],
                R_ones64[0:1, 0:512], start=True, stop=False))
        t_mm = 0
        for k in range(NE):
            for nn in range(4):
                t_mm = te.emit(lambda e, j=j, nn=nn, kb4=kb4, k=k: e.matmul(
                    ps[kb4[nn]][:, 0:512],
                    QY[:, k * 1024 + j * 128:k * 1024 + (j + 1) * 128],
                    xnT[:, k * 2048 + nn * 512:k * 2048 + (nn + 1) * 512],
                    start=False, stop=(k == NE - 1)),
                    1 if (k == NE - 1 and nn == 3) else 0)
        for nn in range(4):
            t_k_done = rope_evac(ps[kb4[nn]][:, 0:512],
                                 KT[:, j * 2048 + nn * 512:j * 2048 + (nn + 1) * 512],
                                 512, COSIN[:, nn * 512:(nn + 1) * 512], t_mm)
            ps_free[kb4[nn]] = [(ve, t_k_done)]

    t_v_done = 0
    for j in range(NKV):
        vb4 = [(j % 2) * 4 + i for i in range(4)]
        for b in vb4:
            te_wait_free(b)
        for nn in range(4):
            te.emit(lambda e, j=j, nn=nn, vb4=vb4: e.matmul(
                ps[vb4[nn]][:, 0:512], R_bkv[0:1, 512 + j * 128:512 + (j + 1) * 128],
                R_ones64[0:1, 0:512], start=True, stop=False))
        t_mm = 0
        for k in range(NE):
            for nn in range(4):
                t_mm = te.emit(lambda e, j=j, nn=nn, vb4=vb4, k=k: e.matmul(
                    ps[vb4[nn]][:, 0:512],
                    QY[:, k * 1024 + 512 + j * 128:k * 1024 + 512 + (j + 1) * 128],
                    xnT[:, k * 2048 + nn * 512:k * 2048 + (nn + 1) * 512],
                    start=False, stop=(k == NE - 1)),
                    1 if (k == NE - 1 and nn == 3) else 0)
        t_vt = 0
        for nn in range(4):
            ve.wait(te, t_mm)
            t_vt = ve.emit(lambda e, nn=nn, vb4=vb4: e.tensor_copy(
                XNTB[:, nn * 512:(nn + 1) * 512], ps[vb4[nn]][:, 0:512]), 1)
            ps_free[vb4[nn]] = [(ve, t_vt)]
        trb = (4 + (j % 2), 5 - (j % 2))
        te.wait(ve, t_vt)
        tok_half = {}
        for half in range(2):
            b = trb[half]
            te_wait_free(b)
            tt = 0
            for i in range(8):
                c = half * 8 + i
                tt = te.emit(lambda e, b=b, i=i, c=c: e.transpose(
                    psb[b][:, i * 128:(i + 1) * 128],
                    XNTB[:, c * 128:(c + 1) * 128], IDN[:, :]), 1 if i == 7 else 0)
            tok_half[half] = tt
        for half in range(2):
            b = trb[half]
            ve.wait(te, tok_half[half])
            t_v_done = ve.emit(lambda e, b=b, half=half, j=j: e.tensor_copy(
                VT[:, j * 2048 + half * 1024:j * 2048 + (half + 1) * 1024],
                psb[b][:, 0:1024]), 1)
            ps_free[b] = [(ve, t_v_done)]

    wq_free = [(None, 0), (None, 0)]
    stg_free3 = [(se, wkv_cast), (se, wkv_cast)]
    for h in range(NH):
        wsl = h % 2
        eng, cnt = wq_free[wsl]
        if eng is not None and cnt:
            se.wait(eng, cnt)
        t_cast = 0
        for half in range(2):
            ssl = (h * 2 + half) % 2
            eng2, cnt2 = stg_free3[ssl]
            if eng2 is not None and cnt2:
                sy.wait(eng2, cnt2)
            t_ld = 0
            for i in range(8):
                k = half * 8 + i
                t_ld = sy.emit(lambda e, h=h, k=k, i=i, ssl=ssl: e.dma_start(
                    ASTG[ssl][:, i * 128:(i + 1) * 128],
                    wq_d[k * 128:(k + 1) * 128, h * 128:(h + 1) * 128]), 16)
            se.wait(sy, t_ld)
            t_cast = se.emit(lambda e, half=half, ssl=ssl, wsl=wsl: e.activation(
                AWQ[wsl][:, half * 1024:(half + 1) * 1024], ASTG[ssl][:, :],
                AF.Copy, scale=0.08838834764831845), 1)
            stg_free3[ssl] = (se, t_cast)
        t_q_mm = 0
        qb4 = [(h % 2) * 4 + i for i in range(4)]
        te.wait(se, t_cast)
        for b in qb4:
            te_wait_free(b)
        for nn in range(4):
            te.emit(lambda e, h=h, nn=nn, qb4=qb4: e.matmul(
                ps[qb4[nn]][:, 0:512], R_bq[0:1, h * 128:(h + 1) * 128],
                R_ones0[0:1, 0:512], start=True, stop=False))
        for k in range(NE):
            for nn in range(4):
                t_q_mm = te.emit(lambda e, nn=nn, k=k, wsl=wsl, qb4=qb4: e.matmul(
                    ps[qb4[nn]][:, 0:512], AWQ[wsl][:, k * 128:(k + 1) * 128],
                    xnT[:, k * 2048 + nn * 512:k * 2048 + (nn + 1) * 512],
                    start=False, stop=(k == NE - 1)),
                    1 if (k == NE - 1 and nn == 3) else 0)
        for nn in range(4):
            tk = rope_evac(ps[qb4[nn]][:, 0:512],
                           QY[:, h * 2048 + nn * 512:h * 2048 + (nn + 1) * 512],
                           512, COSIN[:, nn * 512:(nn + 1) * 512], t_q_mm)
            ps_free[qb4[nn]] = [(ve, tk)]
        wq_free[wsl] = (te, t_q_mm)

    if stages <= 3:
        gp.wait(ve, ve.n)
        _finish_debug()
        return _assemble()

    t_q_final = ve.n
    se.wait(ve, t_q_final)
    te.wait(ve, t_q_final)
    ASC2 = [AR[:, 0:672], AR[:, 672:1344], AR[:, 1344:2016], AR[:, 2016:2688]]
    AEX2 = [AR[:, 2688:3360], AR[:, 3360:4032], AR[:, 4032:4704], AR[:, 4704:5376]]
    PBUF2 = [SLAB[:, 0:768], SLAB[:, 768:1536]]
    PTB2 = [SLAB[:, 1536:2304], SLAB[:, 2304:3072]]
    iters = [(h, qb) for h in range(NH) for qb in range(NT)]
    N = len(iters)
    st_ = [dict() for _ in range(N)]

    def geom(i):
        h, qb = iters[i]
        j = h // 4
        if qb < 4:
            return h, qb, j, (qb + 1) * 128, qb * 128, 0, False
        return h, qb, j, (656 if qb >= 5 else 640), 512, qb - 4, (qb >= 5)

    def emit_S(i):
        h, qb, j, ext, nmid, klo, meta = geom(i)
        par = i % 2
        bA, bB = par, 2 + par
        te_wait_free(bA)
        te_wait_free(bB)
        qcol = h * 2048 + qb * 128
        t_sA = t_sB = 0
        if qb >= 4:
            t_sB = te.emit(lambda e: e.matmul(
                ps[bB][:, 0:128], QY[:, qcol:qcol + 128],
                KT[:, j * 2048 + klo * 128:j * 2048 + (klo + 1) * 128],
                start=True, stop=True), 1)
            t_sA = te.emit(lambda e: e.matmul(
                ps[bA][:, 0:384], QY[:, qcol:qcol + 128],
                KT[:, j * 2048 + (klo + 1) * 128:j * 2048 + (klo + 4) * 128],
                start=True, stop=True), 1)
        elif nmid > 0:
            t_sA = te.emit(lambda e: e.matmul(
                ps[bA][:, 0:nmid], QY[:, qcol:qcol + 128],
                KT[:, j * 2048:j * 2048 + nmid], start=True, stop=True), 1)
        t_sB = te.emit(lambda e: e.matmul(
            ps[bB][:, 128:256], QY[:, qcol:qcol + 128],
            KT[:, j * 2048 + qb * 128:j * 2048 + (qb + 1) * 128],
            start=True, stop=True), 1)
        if meta:
            t_sA = te.emit(lambda e: e.matmul(
                ps[bA][:, 384:400], QY[:, qcol:qcol + 128],
                KT[:, j * 2048:j * 2048 + 16], start=True, stop=True), 1)
        st_[i]["sA"] = t_sA
        st_[i]["sB"] = t_sB

    def emit_E(i):
        h, qb, j, ext, nmid, klo, meta = geom(i)
        par = i % 2
        par4 = i % 4
        bA, bB = par, 2 + par
        asc = ASC2[par4]
        if i >= 4:
            se.wait(ve, st_[i - 4]["nm"])
            ve.wait(se, st_[i - 4]["exp"])
        ev_se = ev_ve = 0
        if qb >= 4:
            mask_ap = MASKS[:, 256:384] if qb == 4 else MASKS[:, 128:256]
            ve.wait(te, st_[i]["sB"])
            ve.emit(lambda e: e.tensor_add(asc[:, 0:128], ps[bB][:, 0:128], mask_ap))
            ev_ve = ve.emit(lambda e: e.tensor_add(
                asc[:, 512:640], ps[bB][:, 128:256], MASKS[:, 0:128]), 1)
            se.wait(te, st_[i]["sA"])
            ev_se = se.emit(lambda e: e.copy(asc[:, 128:512], ps[bA][:, 0:384]), 1)
            if meta:
                ev_se = se.emit(lambda e: e.copy(asc[:, 640:656], ps[bA][:, 384:400]), 1)
            ps_free[bA] = [(se, ev_se)]
            ps_free[bB] = [(ve, ev_ve)]
        else:
            if nmid > 0:
                se.wait(te, st_[i]["sA"])
                ev_se = se.emit(lambda e: e.copy(asc[:, 0:nmid], ps[bA][:, 0:nmid]), 1)
                ps_free[bA] = [(se, ev_se)]
            ve.wait(te, st_[i]["sB"])
            ev_ve = ve.emit(lambda e: e.tensor_add(
                asc[:, nmid:nmid + 128], ps[bB][:, 128:256], MASKS[:, 0:128]), 1)
            ps_free[bB] = [(ve, ev_ve)]
        if ev_se:
            ve.wait(se, ev_se)
        smc = 4 + par4 * 3
        t_nm = ve.emit(lambda e: e.reduce_max(
            SM[:, smc:smc + 1], asc[:, 0:ext], axis=AX.X, negate=True), 1)
        st_[i]["nm"] = t_nm

    def emit_X(i):
        h, qb, j, ext, nmid, klo, meta = geom(i)
        par4 = i % 4
        smc = 4 + par4 * 3
        se.wait(ve, st_[i]["nm"])
        if i >= 4:
            se.wait(ve, st_[i - 4]["div"])
        t_exp = se.emit(lambda e: e.activation(
            AEX2[par4][:, 0:ext], ASC2[par4][:, 0:ext], AF.Exp,
            bias=SM[:, smc:smc + 1], accum_out=SM[:, smc + 1:smc + 2]), 1)
        st_[i]["exp"] = t_exp

    def emit_T(i):
        h, qb, j, ext, nmid, klo, meta = geom(i)
        par = i % 2
        par4 = i % 4
        smc = 4 + par4 * 3
        bT = 4 + par
        bY = 6 + ((qb // 4) % 2)
        nblk = ext // 128
        ve.wait(se, st_[i]["exp"])
        t_rc = ve.emit(lambda e: e.reciprocal(
            SM[:, smc + 2:smc + 3], SM[:, smc + 1:smc + 2]), 1)
        ve.wait(ve, t_rc)
        if i >= 2:
            ve.wait(te, st_[i - 2]["pt"])
        t_div = ve.emit(lambda e: e.tensor_scalar(
            PBUF2[par][:, 0:ext], AEX2[par4][:, 0:ext], SM[:, smc + 2:smc + 3],
            None, ALU.mult), 1)
        st_[i]["div"] = t_div
        te.wait(ve, t_div)
        te_wait_free(bT)
        t_tr = 0
        for b_ in range(nblk):
            t_tr = te.emit(lambda e, b_=b_: e.transpose(
                psb[bT][:, b_ * 128:(b_ + 1) * 128],
                PBUF2[par][:, b_ * 128:(b_ + 1) * 128], IDN[:, :]),
                1 if (b_ == nblk - 1 and not meta) else 0)
        if meta:
            t_tr = te.emit(lambda e: e.transpose(
                psb[bT][0:16, 640:768], PBUF2[par][:, 640:656], IDN[:, :]), 1)
        st_[i]["pt"] = t_tr
        ve.wait(te, t_tr)
        if i >= 2:
            ve.wait(te, st_[i - 2]["pv"])
        w = nblk * 128 + (128 if meta else 0)
        t_ptc = ve.emit(lambda e: e.tensor_copy(PTB2[par][:, 0:w],
                                                psb[bT][:, 0:w]), 1)
        ps_free[bT] = [(ve, t_ptc)]
        te.wait(ve, t_ptc)
        if qb % 4 == 0:
            te_wait_free(bY)
        yoff = (qb % 4) * 128
        t_pv = 0
        for b_ in range(nblk):
            ktile = klo + b_ if qb >= 4 else b_
            t_pv = te.emit(lambda e, b_=b_, ktile=ktile: e.matmul(
                ps[bY][:, yoff:yoff + 128],
                VT[:, j * 2048 + ktile * 128:j * 2048 + (ktile + 1) * 128],
                PTB2[par][:, b_ * 128:(b_ + 1) * 128],
                start=(b_ == 0), stop=(b_ == nblk - 1 and not meta)),
                1 if (b_ == nblk - 1 and not meta) else 0)
        if meta:
            t_pv = te.emit(lambda e: e.matmul(
                ps[bY][:, yoff:yoff + 128], VT[0:16, j * 2048:j * 2048 + 128],
                PTB2[par][0:16, nblk * 128:nblk * 128 + 128],
                start=False, stop=True), 1)
        st_[i]["pv"] = t_pv
        if qb % 4 == 3:
            ve.wait(te, t_pv)
            tk = ve.emit(lambda e: e.tensor_copy(
                QY[:, h * 2048 + (qb - 3) * 128:h * 2048 + (qb + 1) * 128],
                ps[bY][:, 0:512]), 1)
            ps_free[bY] = [(ve, tk)]
            st_[i]["ye"] = tk
        return st_[i].get("ye", 0)

    t_y_done = 0
    for r in range(N // 2 + 4):
        for sub in range(2):
            i = 2 * r + sub - 2
            if 0 <= i < N:
                emit_E(i)
        for sub in range(2):
            i = 2 * r + sub
            if i < N:
                emit_S(i)
        for sub in range(2):
            i = 2 * r + sub - 4
            if 0 <= i < N:
                emit_X(i)
        for sub in range(2):
            i = 2 * r + sub - 6
            if 0 <= i < N:
                tk = emit_T(i)
                if tk:
                    t_y_done = tk

    for a_ in (te, ve, se, gp, sy):
        for b_ in (te, ve, se, gp, sy):
            if a_ is not b_:
                a_.wait(b_, b_.n)
    t_msb = se.emit(lambda e: e.copy(MSB[:, :], MS[:, :]), 1)
    te.wait(se, t_msb)
    te_wait_free(4)
    t_mst = te.emit(lambda e: e.transpose(psb[4][0:32, 0:128], MSB[:, :],
                                          IDN[:, :]), 1)
    ve.wait(te, t_mst)
    t_mstc = ve.emit(lambda e: e.tensor_copy(MST[:, :], psb[4][0:32, 0:128]), 1)
    ps_free[4] = [(ve, t_mstc)]
    sy.wait(ve, t_mstc)
    t_msr = 0
    for t in range(NT):
        sy.emit(lambda e, t=t: e.dma_start(R_m[0:1, t * 128:(t + 1) * 128],
                                           MST[t:t + 1, :]), 16)
        t_msr = sy.emit(lambda e, t=t: e.dma_start(
            R_sd[0:1, t * 128:(t + 1) * 128], MST[16 + t:17 + t, :]), 16)
    te.wait(sy, t_msr)
    sd_toks = []
    for i in range(4):
        b = 4 + i
        te_wait_free(b)
        sd_toks.append(te.emit(lambda e, b=b, i=i: e.matmul(
            ps[b][:, 0:512], R_ones0[0:1, 0:128],
            R_sd[0:1, i * 512:(i + 1) * 512], start=True, stop=True), 1))
    t_sdr = 0
    for i in range(4):
        b = 4 + i
        ve.wait(te, sd_toks[i])
        t_sdr = ve.emit(lambda e, b=b, i=i: e.tensor_copy(
            ASDR[:, i * 512:(i + 1) * 512], ps[b][:, 0:512]), 1)
        ps_free[b] = [(ve, t_sdr)]

    te.wait(ve, t_y_done)
    if stages <= 4.2:
        gp.wait(ve, t_sdr)
        _finish_debug()
        return _assemble()

    wo_free = [(None, 0), (None, 0)]
    t_x2_done = 0
    for f in range(int(NE) if stages > 4.5 else 1):
        wsl = f % 2
        eng, cnt = wo_free[wsl]
        if eng is not None and cnt:
            sy.wait(eng, cnt)
        t_ld = 0
        for k in range(NE):
            t_ld = sy.emit(lambda e, f=f, k=k, wsl=wsl: e.dma_start(
                AXS[wsl][:, k * 128:(k + 1) * 128],
                wo_d[k * 128:(k + 1) * 128, f * 128:(f + 1) * 128]), 16)
        se.wait(sy, t_ld)
        t_woc = se.emit(lambda e, wsl=wsl: e.copy(AWO[wsl][:, :],
                                                  AXS[wsl][:, 0:2048]), 1)
        wo_free[wsl] = (se, t_woc)
        g = (f % 2) * 4
        te.wait(se, t_woc)
        t_wo_mm = 0
        gb4 = [g + i for i in range(4)]
        for b in gb4:
            te_wait_free(b)
        for nn in range(4):
            te.emit(lambda e, f=f, nn=nn, gb4=gb4: e.matmul(
                ps[gb4[nn]][:, 0:512], R_bo[0:1, f * 128:(f + 1) * 128],
                R_ones0[0:1, 0:512], start=True, stop=False))
            te.emit(lambda e, nn=nn, gb4=gb4: e.matmul(
                ps[gb4[nn]][:, 0:512], R_ones0[0:1, 0:128],
                R_m[0:1, nn * 512:(nn + 1) * 512], start=False, stop=False))
        for k in range(NE):
            for nn in range(4):
                t_wo_mm = te.emit(lambda e, nn=nn, gb4=gb4, k=k, wsl=wsl: e.matmul(
                    ps[gb4[nn]][:, 0:512], AWO[wsl][:, k * 128:(k + 1) * 128],
                    QY[:, k * 2048 + nn * 512:k * 2048 + (nn + 1) * 512],
                    start=False, stop=(k == NE - 1)),
                    1 if (k == NE - 1 and nn == 3) else 0)
        mm = {nn: t_wo_mm for nn in range(4)}
        wo_free[wsl] = (te, t_wo_mm)
        for nn in range(4):
            bank = g + nn
            ve.wait(te, mm[nn])
            if f == 0 and nn == 0:
                ve.wait(ve, t_sdr)
            cl, ch = f * 2048 + nn * 512, f * 2048 + (nn + 1) * 512
            ve.emit(lambda e, cl=cl, ch=ch, nn=nn: e.tensor_tensor(
                SLAB[:, 0:512], xnT[:, cl:ch],
                ASDR[:, nn * 512:(nn + 1) * 512], ALU.mult))
            t_x2_done = ve.emit(lambda e, bank=bank, cl=cl, ch=ch: e.tensor_add(
                xnT[:, cl:ch], ps[bank][:, 0:512], SLAB[:, 0:512]), 1)
            ps_free[bank] = [(ve, t_x2_done)]

    if stages <= 5:
        gp.wait(ve, t_x2_done)
        _finish_debug()
        return _assemble()

    for a_ in (te, ve, se, gp, sy):
        for b_ in (te, ve, se, gp, sy):
            if a_ is not b_:
                a_.wait(b_, b_.n)
    t_u4_cast = wo_free[0][1] if wo_free[0][0] is not None else 0
    t_u4_cast2 = wo_free[1][1] if wo_free[1][0] is not None else 0
    t_u4_sy = sy.n
    t_srep2 = build_srep(ve, t_x2_done)
    w2_tok = load_wffn((te, te.n))
    out_free = [[(sy, t_u4_sy), (se, se.n)], [(sy, t_u4_sy), (se, se.n)]]
    u5 = [dict() for _ in range(NT)]
    st_tok = 0

    def u5_A(t):
        tok_half = {}
        for half in range(2):
            b = 4 + half
            te_wait_free(b)
            te.wait(ve, t_x2_done)
            t_tr = 0
            for i in range(8):
                f = half * 8 + i
                t_tr = te.emit(lambda e, b=b, i=i, f=f, t=t: e.transpose(
                    psb[b][:, i * 128:(i + 1) * 128],
                    xnT[:, f * 2048 + t * 128:f * 2048 + (t + 1) * 128],
                    IDN[:, :]), 1 if i == 7 else 0)
            tok_half[half] = t_tr
        t_x2t = 0
        for half in range(2):
            b = 4 + half
            ve.wait(te, tok_half[half])
            t_x2t = ve.emit(lambda e, b=b, half=half: e.tensor_copy(
                SLAB[:, half * 1024:(half + 1) * 1024], psb[b][:, 0:1024]), 1)
            ps_free[b] = [(ve, t_x2t)]
        sq_bank = 6 if t % 2 == 0 else 7
        t_r3, t_sq3, _ = ln_stats(SLAB[:, 0:2048], sq_bank, 0, 1, [(ve, t_x2t)])
        if t >= 1:
            ve.wait(te, u5[t - 1]["tr3te"])
        u5[t]["xn3"] = ve.emit(lambda e: e.tensor_scalar(
            SLAB[:, 2048:4096], SLAB[:, 0:2048], SM[:, 0:1], SM[:, 1:2],
            ALU.subtract, ALU.mult), 1)

    def u5_B(t):
        nonlocal_box = None
        t_tr3 = transpose_tile(SLAB[:, 2048:4096], XNTB, lambda b: b * 128,
                               ve, u5[t]["xn3"])
        u5[t]["tr3te"] = te.n
        sl = t % 2
        for eng, tok in out_free[sl]:
            ve.wait(eng, tok)
        t_o = ffn_tile(t, None, AXS[sl], SLAB[:, 0:2048], w2_tok, t_tr3)
        gp.wait(ve, t_o)
        tok = gp.emit(lambda e, t=t, sl=sl: e.dma_start(
            out_d[t * 128:(t + 1) * 128, :], AXS[sl][:, :]), 16)
        out_free[sl] = [(gp, tok)]
        return tok

    for r in range(NT + 1):
        if r >= 1:
            st_tok = u5_B(r - 1)
        if r < NT:
            u5_A(r)

    gp.wait(gp, st_tok)
    return _assemble()


def _host_prep(inputs):
    perm = np.concatenate([np.arange(0, 128, 2), np.arange(1, 128, 2)])
    permE = np.concatenate([h * 128 + perm for h in range(16)])
    perm512 = np.concatenate([j * 128 + perm for j in range(4)])

    wq = np.ascontiguousarray(inputs["L0_attn_Wq"][:, permE])
    wkv = inputs["L0_attn_Wkv"].copy()
    wkv[:, 0:512] = wkv[:, perm512]
    wkv = np.ascontiguousarray(wkv)
    bq = np.ascontiguousarray(inputs["L0_attn_bq"][permE]).reshape(1, -1)
    bkv = inputs["L0_attn_bkv"].copy()
    bkv[0:512] = bkv[perm512]
    bkv = bkv.reshape(1, -1)
    wffnT = np.ascontiguousarray(inputs["L0_ffn_weight"].T)
    wo = np.ascontiguousarray(inputs["L0_attn_Wo"])

    inv = (1.0 / 10000.0 ** (np.arange(0, 128, 2, dtype=np.float64) / 128.0))
    ang = inv[:, None] * np.arange(S, dtype=np.float64)[None, :]
    cosin = np.concatenate([np.cos(ang), np.sin(ang)], axis=0).astype(np.float32)

    q = np.arange(128)[:, None]
    k = np.arange(128)[None, :]
    mL = np.where(k <= q, 0.0, NEG).astype(np.float32)
    mU = np.where(k >= q + 1, 0.0, NEG).astype(np.float32)
    mMU = np.where((k >= q + 1) | (k < 16), 0.0, NEG).astype(np.float32)
    masks = np.concatenate([mL, mU, mMU], axis=1).astype(ml_dtypes.bfloat16)

    idn = np.eye(128, dtype=ml_dtypes.bfloat16)

    shared = {
        "wffnT": wffnT, "wq": wq, "wkv": wkv, "wo": wo,
        "fscale": inputs["L0_ffn_scale_weight"].reshape(1, -1).astype(np.float32),
        "fbias": inputs["L0_ffn_bias"].reshape(1, -1).astype(np.float32),
        "bq": bq.astype(np.float32), "bkv": bkv.astype(np.float32),
        "bo": inputs["L0_attn_bo"].reshape(1, -1).astype(np.float32),
        "cosin": cosin, "masks": masks, "idn": idn,
    }
    return shared


def kernel(**inputs):
    shared = _host_prep(inputs)
    x = np.ascontiguousarray(inputs["L0_attn_x"], dtype=np.float32)
    nc = build_program()
    in_maps = [dict(shared, x=np.ascontiguousarray(x[b])) for b in range(8)]
    res = run_bass_kernel_spmd(nc, in_maps, core_ids=list(range(8)))
    out = np.stack([np.asarray(r["out"]) for r in res.results], axis=0)
    return out.astype(np.float32)
